# version 20
# baseline (speedup 1.0000x reference)
import numpy as np
import contextlib
import concourse.bass as bass
import concourse.mybir as mybir
from concourse.bass_utils import run_bass_kernel_spmd

F32 = mybir.dt.float32
BF16 = mybir.dt.bfloat16
I8 = mybir.dt.int8
Alu = mybir.AluOpType
Act = mybir.ActivationFunctionType
AX = mybir.AxisListType

NCORES = 8
NT = 64
D = 1024
TPC = NT * 128
EPS_LN = 1e-5
EPS_Q = 1e-6
CORD_STEPS = [64.0, 32.0, 16.0, 8.0, 4.0, 2.0, 1.0, 0.5]
NC_ = len(CORD_STEPS)
NSEG = 5

CORD = [(f"csgn{k}", "S") for k in range(1, NC_ + 1)]

ENG_OF = {
    "ln1stat": "V", "ln1r": "S", "ln1b": "V",
    "ln1n": "S", "ln1T": "T", "ln1fmev": "S",
    "qkv_q": "T", "qkv_k": "T", "qkv_v": "T",
    "qev": "S", "kev": "S", "vev": "S",
    "rope": "V", "qT": "T", "kT": "T",
    "qfmev": "S", "kfmev": "S", "sall": "T",
    "exp": "S", "pnorm": "V", "pt": "T", "ptev": "S", "pv": "T",
    "yev": "S", "proj": "T", "lnastat": "V", "lnar": "S", "lnab": "V", "lnan": "S", "r1": "G", "load": "D",
    "ln2stat": "V", "ln2r": "S", "qsc": "V",
    "qfin": "V", "xqT": "T", "xqfmev": "S", "ffn": "T",
    "deqs": "S", "deq": "V",
    "lnfstat": "V", "lnfr": "S", "lnfb": "V",
    "lnfn": "S", "r2": "G", "store": "D",
}
for k in range(1, NC_ + 1):
    ENG_OF[f"csgn{k}"] = "S"
    ENG_OF[f"cord{k}"] = "V"

DEPS = {
    "load": [("r1", -4)],
    "ln1stat": [("load", 0), ("ln1b", -2)],
    "ln1r": [("ln1stat", 0), ("ln1n", -2)],
    "ln1b": [("ln1r", 0), ("ln1n", -2)],
    "ln1n": [("ln1b", 0), ("ln1r", 0), ("ln1T", -1)],
    "ln1T": [("ln1n", 0), ("deqs", -5), ("ln1fmev", -1)],
    "ln1fmev": [("ln1T", 0), ("qkv_q", -2), ("qkv_k", -2), ("qkv_v", -2), ("rope", -2)],
    "qkv_q": [("ln1fmev", 0), ("exp", -2), ("qev", -1)],
    "qkv_k": [("ln1fmev", 0), ("lnastat", -3), ("lnan", -3), ("kev", -1)],
    "qkv_v": [("ln1fmev", 0), ("yev", -3), ("vev", -1)],
    "qev": [("qkv_q", 0), ("rope", -2)],
    "kev": [("qkv_k", 0), ("rope", -2)],
    "vev": [("qkv_v", 0), ("pv", -3)],
    "rope": [("qev", 0), ("kev", 0), ("qkv_v", 0), ("qT", -1), ("kT", -1)],
    "qT": [("rope", 0), ("qev", 1), ("exp", -1)],
    "kT": [("rope", 0), ("kev", 1), ("lnan", -2)],
    "qfmev": [("qT", 0), ("sall", -1)],
    "kfmev": [("kT", 0), ("sall", -1)],
    "sall": [("qfmev", 0), ("kfmev", 0), ("exp", -1)],
    "exp": [("sall", 0), ("pnorm", -1)],
    "pnorm": [("exp", 0), ("pt", -2)],
    "pt": [("pnorm", 0), ("vev", 2), ("ptev", -1), ("yev", -1)],
    "ptev": [("pt", 0), ("pv", -1)],
    "pv": [("ptev", 0), ("vev", 0)],
    "yev": [("pv", 0), ("proj", -1)],
    "proj": [("yev", 0), ("kfmev", 1), ("lnan", -1)],
    "lnastat": [("proj", 0), ("lnab", -2)],
    "lnar": [("lnastat", 0), ("lnan", -2)],
    "lnab": [("lnar", 0), ("lnan", -2)],
    "lnan": [("lnab", 0), ("lnar", 0), ("r1", -1)],
    "r1": [("lnan", 0), ("r2", -3)],
    "ln2stat": [("r1", 0), ("qsc", -2)],
    "ln2r": [("ln2stat", 0), (f"cord{NC_}", -2), ("qsc", -2)],
    "qsc": [("ln2r", 0), ("qfin", -2), ("deqs", -2), ("deq", -2)],
    "csgn1": [("qsc", 0)],
    "qfin": [(f"cord{NC_}", 0), ("xqT", -2)],
    "xqT": [("qfin", 0), ("ln1fmev", 4), ("xqfmev", -1), ("deqs", -1)],
    "xqfmev": [("xqT", 0), ("ffn", -1)],
    "ffn": [("xqfmev", 0)],
    "deqs": [("ffn", 0), ("r2", -1)],
    "deq": [("deqs", 0), ("r2", -1)],
    "lnfstat": [("deq", 0), ("lnfb", -2)],
    "lnfr": [("lnfstat", 0), ("lnfn", -2)],
    "lnfb": [("lnfr", 0), ("lnfn", -2)],
    "lnfn": [("lnfb", 0), ("lnfr", 0), ("r2", -1)],
    "r2": [("lnfn", 0), ("store", -2)],
    "store": [("r2", 0)],
}
for k in range(1, NC_ + 1):
    if k > 1:
        DEPS[f"csgn{k}"] = [(f"cord{k - 1}", 0)]
    DEPS[f"cord{k}"] = [(f"csgn{k}", 0)]

DUR = {
    "load": 700, "store": 700, "ln1stat": 1600,
    "ln1r": 560, "ln1n": 1260, "ln1T": 1400, "ln1fmev": 1120,
    "qkv_q": 3700, "qkv_k": 3700, "qkv_v": 3700,
    "qev": 1120, "kev": 1120, "vev": 1120,
    "rope": 4100, "qT": 1300, "kT": 1300,
    "qfmev": 1120, "kfmev": 1120, "sall": 2600,
    "exp": 1150, "pnorm": 2900, "pt": 1800, "ptev": 1120,
    "pv": 2600, "yev": 1120, "proj": 4600, "lnastat": 1700,
    "lnar": 560, "lnan": 1260, "r1": 2300, "ln2stat": 1800,
    "ln2r": 2600, "qsc": 2400, "qfin": 1250,
    "xqT": 1400, "xqfmev": 1120, "ffn": 4200,
    "deqs": 1100, "deq": 1900, "lnfstat": 1600,
    "lnfr": 560, "lnfn": 1260, "r2": 2300,
}
for k in range(1, NC_ + 1):
    DUR[f"csgn{k}"] = 1070
    DUR[f"cord{k}"] = 1250

ALL_STAGES = list(ENG_OF.keys())
SEM_LAT = 1100.0


def list_schedule(nt):
    import heapq
    nodes = [(n, t) for t in range(nt) for n in ALL_STAGES]
    ndeps = {}
    dependents = {}
    def dep_tile(t, dt):
        u = t + dt
        if dt > 0:
            u = min(u, nt - 1)
        return u if 0 <= u < nt else None

    for n, t in nodes:
        k = 0
        seen = set()
        for dn, dt in DEPS.get(n, []):
            u = dep_tile(t, dt)
            if u is not None and (dn, u) != (n, t) and (dn, u) not in seen:
                seen.add((dn, u))
                k += 1
                dependents.setdefault((dn, u), []).append((n, t))
        ndeps[(n, t)] = k
    blevel = {}
    outdeg = {nd: len(dependents.get(nd, [])) for nd in nodes}
    stack = [nd for nd in nodes if outdeg[nd] == 0]
    for nd in stack:
        blevel[nd] = DUR.get(nd[0], 500)
    while stack:
        nd = stack.pop()
        for dn, dt in DEPS.get(nd[0], []):
            u = nd[1] + dt
            if 0 <= u < nt:
                p = (dn, u)
                blevel[p] = max(blevel.get(p, 0),
                                DUR.get(dn, 500) + blevel[nd])
                outdeg[p] -= 1
                if outdeg[p] == 0:
                    stack.append(p)
    eng_avail = {"V": 0.0, "S": 0.0, "T": 0.0, "G": 0.0, "D": 0.0}
    ready = {e: [] for e in eng_avail}
    rtime = {}
    finish = {}
    pending = []
    seq = 0
    for n, t in nodes:
        if ndeps[(n, t)] == 0:
            heapq.heappush(pending, (0.0, seq, n, t)); seq += 1
    order = {e: [] for e in eng_avail}
    n_done = 0
    clock = 0.0
    while n_done < len(nodes):
        moved = False
        while pending and pending[0][0] <= clock + 1e-9:
            rt, _, n, t = heapq.heappop(pending)
            e = ENG_OF[n]
            heapq.heappush(ready[e], (-blevel.get((n, t), 0), t, seq, n))
            seq += 1
            rtime[(n, t)] = rt
            moved = True
        progressed = False
        for e in eng_avail:
            if ready[e] and eng_avail[e] <= clock + 1e-9:
                _, t, _, n = heapq.heappop(ready[e])
                start = max(eng_avail[e], rtime[(n, t)])
                fin = start + DUR.get(n, 500)
                eng_avail[e] = fin
                finish[(n, t)] = fin
                order[e].append((n, t))
                n_done += 1
                progressed = True
                for (cn, cu) in dependents.get((n, t), []):
                    ndeps[(cn, cu)] -= 1
                    if ndeps[(cn, cu)] == 0:
                        dr = 0.0
                        for dn2, dt2 in DEPS.get(cn, []):
                            u2 = dep_tile(cu, dt2)
                            if u2 is not None and (dn2, u2) != (cn, cu):
                                dr = max(dr, finish[(dn2, u2)] + SEM_LAT)
                        heapq.heappush(pending, (dr, seq, cn, cu)); seq += 1
        if not progressed and not moved:
            cands = []
            if pending:
                cands.append(pending[0][0])
            for e in eng_avail:
                if ready[e]:
                    cands.append(eng_avail[e])
            if not cands:
                raise RuntimeError("schedule stuck")
            clock = max(min(cands), clock + 1.0)
    span = max(finish.values())
    return order, span


def plan_marks(nt, order):
    cnt = {"V": 1, "S": 0, "T": 0, "G": 0}
    marks = {}
    for e in ("V", "S", "T", "G"):
        c = cnt[e]
        for n, t in order[e]:
            c += 1
            marks[(n, t)] = c
    return marks


N_WDMA_G = 25
N_WDMA_S = 4


def build_nc(nt=NT, detect_races=False):
    tpc = nt * 128
    order, est_span = list_schedule(nt)
    marks = plan_marks(nt, order)
    nc = bass.Bass("TRN2", target_bir_lowering=False,
                   detect_race_conditions=detect_races)
    x_d = nc.dram_tensor("x", [tpc, D], F32, kind="ExternalInput").ap()
    wqkv_d = nc.dram_tensor("wqkvT", [D, 3 * D], F32, kind="ExternalInput").ap()
    wproj_d = nc.dram_tensor("wprojT", [D, D], F32, kind="ExternalInput").ap()
    wq_d = nc.dram_tensor("wqT", [D, D], I8, kind="ExternalInput").ap()
    wsb_d = nc.dram_tensor("wsb", [128, D], F32, kind="ExternalInput").ap()
    biasb_d = nc.dram_tensor("biasb", [128, D], F32, kind="ExternalInput").ap()
    ctab_d = nc.dram_tensor("ctab", [128, 512], F32, kind="ExternalInput").ap()
    stab_d = nc.dram_tensor("stab", [128, 512], F32, kind="ExternalInput").ap()
    id_d = nc.dram_tensor("ident", [128, 128], F32, kind="ExternalInput").ap()
    out_d = nc.dram_tensor("out", [tpc, D], F32, kind="ExternalOutput").ap()

    with contextlib.ExitStack() as es:
        def sb(name, shape, dt):
            return es.enter_context(nc.sbuf_tensor(name, shape, dt))

        def ring(name, n, shape, dt):
            return [sb(f"{name}{i}", shape, dt) for i in range(n)]

        wqkv_b = sb("wqkv_b", [128, 8 * 3 * D], BF16)
        wproj_b = sb("wproj_b", [128, 8 * D], BF16)
        wq_b = sb("wq_b", [128, 8 * D], BF16)
        wsb_s = sb("wsb_s", [128, D], F32)
        biasb_s = sb("biasb_s", [128, D], BF16)
        cosT = sb("cosT", [128, 512], BF16)
        sinT = sb("sinT", [128, 512], BF16)
        Ib = sb("Ib", [128, 128], BF16)
        c127 = sb("c127", [128, 1], F32)
        r127 = sb("r127", [128, 1], F32)
        epsq = sb("epsq", [128, 1], F32)

        x_in = ring("x_in", 4, [128, D], F32)
        xout = ring("xout", 2, [128, D], F32)
        ln1_b = sb("ln1_b", [128, D], BF16)
        ln1_fm = ring("ln1_fm", 2, [128, D], BF16)
        q_ev = ring("q_ev", 2, [128, D], BF16)
        k_ev = ring("k_ev", 2, [128, D], BF16)
        v_ev = ring("v_ev", 3, [128, D], BF16)
        qr = sb("qr", [128, D], BF16)
        kr = sb("kr", [128, D], BF16)
        rsc_a = sb("rsc_a", [128, 512], BF16)
        rsc_b = sb("rsc_b", [128, 512], BF16)
        q_fm = sb("q_fm", [128, D], BF16)
        k_fm = sb("k_fm", [128, D], BF16)
        Pexp = sb("Pexp", [128, D], BF16)
        P_b = ring("P_b", 2, [128, D], BF16)
        PT_b = sb("PT_b", [128, D], BF16)
        y_b = sb("y_b", [128, D], BF16)
        lnam = sb("lnam", [128, D], BF16)
        x2 = ring("x2", 3, [128, D], BF16)
        ubuf = ring("ubuf", 2, [128, D], F32)
        e_a = ring("e_a", 2, [128, D], F32)
        e_b = ring("e_b", 2, [128, D], F32)
        sgn = ring("sgn", 2, [128, D], F32)
        xq_b = ring("xq_b", 2, [128, D], BF16)
        xq_fm = sb("xq_fm", [128, D], BF16)
        yf = sb("yf", [128, D], BF16)
        lnfm = sb("lnfm", [128, D], F32)

        bnbuf = sb("bnbuf", [128, 12], F32)
        mv1 = ring("mv1", 2, [128, 2], F32)
        mva = ring("mva", 2, [128, 2], F32)
        mv2 = ring("mv2", 2, [128, 2], F32)
        mvf = ring("mvf", 2, [128, 2], F32)
        vp1 = ring("vp1", 2, [128, 1], F32)
        vpa = ring("vpa", 2, [128, 1], F32)
        vp2 = ring("vp2", 2, [128, 1], F32)
        vpf = ring("vpf", 2, [128, 1], F32)
        l_1 = ring("l_1", 2, [128, 1], F32)
        l_a = ring("l_a", 2, [128, 1], F32)
        l_2 = ring("l_2", 2, [128, 1], F32)
        l_f = ring("l_f", 2, [128, 1], F32)
        r_1 = ring("r_1", 2, [128, 1], F32)
        r_a = ring("r_a", 2, [128, 1], F32)
        r_2 = ring("r_2", 2, [128, 1], F32)
        r_f = ring("r_f", 2, [128, 1], F32)
        b_1 = ring("b_1", 2, [128, 1], F32)
        b_a = ring("b_a", 2, [128, 1], F32)
        b_f2 = ring("b_f2", 2, [128, 1], F32)
        negm2 = ring("negm2", 2, [128, 1], F32)
        admax = sb("admax", [128, 1], F32)
        amax = sb("amax", [128, 1], F32)
        amaxp = sb("amaxp", [128, 1], F32)
        rrecip = sb("rrecip", [128, 1], F32)
        sinv = sb("sinv", [128, 1], F32)
        rs_q = ring("rs_q", 2, [128, 1], F32)
        xs_ = sb("xs_", [128, 1], F32)
        nxs = ring("nxs", 2, [128, 1], F32)
        ssum = sb("ssum", [128, 16], F32)
        rs16 = sb("rs16", [128, 16], F32)

        psQ = es.enter_context(nc.psum_tensor("psQ", [128, D], F32))
        psK = es.enter_context(nc.psum_tensor("psK", [128, D], F32))
        psV = es.enter_context(nc.psum_tensor("psV", [128, D], F32))
        psB = es.enter_context(nc.psum_tensor("psB", [128, D], F32))
        A0 = psQ[:]
        A1 = psK[:]
        A2 = psV[:]

        sdx = es.enter_context(nc.semaphore("sdx"))
        sdo = es.enter_context(nc.semaphore("sdo"))
        sdwg = es.enter_context(nc.semaphore("sdwg"))
        sdws = es.enter_context(nc.semaphore("sdws"))
        sv = es.enter_context(nc.semaphore("sv"))
        ss = es.enter_context(nc.semaphore("ss"))
        st_ = es.enter_context(nc.semaphore("st_"))
        sg = es.enter_context(nc.semaphore("sg"))
        svx = es.enter_context(nc.semaphore("svx"))
        ssx = es.enter_context(nc.semaphore("ssx"))

        SEM = {"V": sv, "S": ss, "T": st_, "G": sg}

        def w(eng, name, t):
            if not (0 <= t < nt):
                return
            if name == "load":
                eng.wait_ge(sdx, 16 * (t + 1))
            elif name == "store":
                eng.wait_ge(sdo, 16 * (t + 1))
            else:
                eng.wait_ge(SEM[ENG_OF[name]], marks[(name, t)])

        def wdeps(eng, name, t):
            for dn, dt in DEPS.get(name, []):
                u = t + dt
                if dt > 0:
                    u = min(u, nt - 1)
                if (dn, u) != (name, t):
                    w(eng, dn, u)

        def sched(engine_code):
            return list(order[engine_code])

        block = es.enter_context(nc.Block())

        @block.sync
        def _(sync):
            for wv, srcv in ((wsb_s[:], wsb_d[:]),
                             (x_in[0][:, 0:512], ctab_d[:]),
                             (x_in[0][:, 512:1024], stab_d[:]),
                             (x_in[1][:, 0:128], id_d[:])):
                sync.dma_start(out=wv, in_=srcv).then_inc(sdws, 16)
            sync.wait_ge(sv, 1)
            lseen = -1
            sseen = -1
            for name, t in sched("D"):
                wdeps(sync, name, t)
                if name == "load":
                    assert t == lseen + 1
                    lseen = t
                    if t >= 1:
                        sync.wait_ge(sdx, 16 * t)
                    sync.dma_start(out=x_in[t % 4][:],
                                   in_=x_d[t * 128:(t + 1) * 128, :]
                                   ).then_inc(sdx, 16)
                else:
                    assert t == sseen + 1
                    sseen = t
                    if t >= 1:
                        sync.wait_ge(sdo, 16 * t)
                    sync.dma_start(out=out_d[t * 128:(t + 1) * 128, :],
                                   in_=xout[t % 2][:]).then_inc(sdo, 16)
            sync.wait_ge(sdo, 16 * nt)

        @block.gpsimd
        def _(gp):
            for c in range(8):
                gp.dma_start(out=wqkv_b[:, c * 3 * D:(c + 1) * 3 * D],
                             in_=wqkv_d[c * 128:(c + 1) * 128, :]
                             ).then_inc(sdwg, 16)
            for c in range(8):
                gp.dma_start(out=wproj_b[:, c * D:(c + 1) * D],
                             in_=wproj_d[c * 128:(c + 1) * 128, :]
                             ).then_inc(sdwg, 16)
            for c in range(8):
                gp.dma_start(out=wq_b[:, c * D:(c + 1) * D],
                             in_=wq_d[c * 128:(c + 1) * 128, :]
                             ).then_inc(sdwg, 16)
            gp.dma_start(out=biasb_s[:], in_=biasb_d[:]).then_inc(sdwg, 16)
            for name, t in sched("G"):
                wdeps(gp, name, t)
                if name == "r1":
                    gp.tensor_tensor(x2[t % 3][:], x_in[t % 4][:], lnam[:],
                                     Alu.add).then_inc(sg, 1)
                else:
                    gp.tensor_tensor(xout[t % 2][:], x2[t % 3][:], lnfm[:],
                                     Alu.add).then_inc(sg, 1)

        @block.vector
        def _(v):
            vxc = [0]

            def vsync(instr):
                vxc[0] += 1
                instr.then_inc(svx, 1)
                v.wait_ge(svx, vxc[0])
            v.wait_ge(sdws, 16 * N_WDMA_S)
            v.tensor_copy(cosT[:], x_in[0][:, 0:512])
            v.tensor_copy(sinT[:], x_in[0][:, 512:1024])
            v.tensor_copy(Ib[:], x_in[1][:, 0:128])
            vsync(v.memset(c127[:], 127.0))
            vsync(v.memset(epsq[:], EPS_Q))
            vsync(v.reciprocal(r127[:], c127[:]))
            v.tensor_scalar_add(r127[:], r127[:], 0.0).then_inc(sv, 1)

            def rv(buf):
                return buf[:].rearrange("p (h d) -> p h d", h=16)

            cv = rv(cosT)
            sv_v = rv(sinT)
            sav = rv(rsc_a)
            sbv = rv(rsc_b)

            v_need_w = [True]
            for name, t in sched("V"):
                p2 = t % 2
                if name == "deq" and v_need_w[0]:
                    v.wait_ge(sdwg, 16 * N_WDMA_G)
                    v_need_w[0] = False
                wdeps(v, name, t)
                if name == "ln1stat":
                    xa = x_in[t % 4][:]
                    v.bn_stats(bnbuf[:, 0:6], xa[:, 0:512])
                    vsync(v.bn_stats(bnbuf[:, 6:12], xa[:, 512:1024]))
                    vsync(v.bn_aggr(mv1[p2][:], bnbuf[:]))
                    v.tensor_scalar_add(vp1[p2][:], mv1[p2][:, 1:2],
                                        EPS_LN).then_inc(sv, 1)
                elif name == "ln1b":
                    v.scalar_tensor_tensor(b_1[p2][:], mv1[p2][:, 0:1], -1.0,
                                           r_1[p2][:], Alu.mult,
                                           Alu.mult).then_inc(sv, 1)
                elif name == "rope":
                    scv = ln1_fm[p2][:, 0:512].rearrange(
                        "p (h f) -> p h f", f=32)
                    sdv = ln1_fm[p2][:, 512:1024].rearrange(
                        "p (h f) -> p h f", f=32)
                    qv_ = rv(q_ev[p2])
                    kv_ = rv(k_ev[p2])
                    qd = rv(qr)
                    kd = rv(kr)
                    q1, q2 = qv_[:, :, 0:32], qv_[:, :, 32:64]
                    k1, k2 = kv_[:, :, 0:32], kv_[:, :, 32:64]
                    v.tensor_tensor(sav, q1, cv, Alu.mult)
                    v.tensor_tensor(sbv, q2, sv_v, Alu.mult)
                    v.tensor_tensor(scv, k1, cv, Alu.mult)
                    v.tensor_tensor(sdv, k2, sv_v, Alu.mult)
                    v.tensor_tensor(qd[:, :, 0:32], sav, sbv, Alu.add)
                    v.tensor_tensor(kd[:, :, 0:32], scv, sdv, Alu.add)
                    v.tensor_tensor(sav, q2, cv, Alu.mult)
                    v.tensor_tensor(sbv, q1, sv_v, Alu.mult)
                    v.tensor_tensor(scv, k2, cv, Alu.mult)
                    v.tensor_tensor(sdv, k1, sv_v, Alu.mult)
                    v.tensor_tensor(qd[:, :, 32:64], sav, sbv, Alu.subtract)
                    ri = v.tensor_tensor(kd[:, :, 32:64], scv, sdv,
                                         Alu.subtract)
                    ri.then_inc(sv, 1)
                elif name == "pnorm":
                    vsync(v.reduce_sum(
                        ssum[:], Pexp[:].rearrange("p (a b) -> p a b", a=16),
                        axis=AX.X))
                    vsync(v.reciprocal(rs16[:], ssum[:]))
                    for b in range(16):
                        pi = v.tensor_scalar_mul(
                            P_b[p2][:, b * 64:(b + 1) * 64],
                            Pexp[:, b * 64:(b + 1) * 64], rs16[:, b:b + 1])
                    pi.then_inc(sv, 1)
                elif name == "lnastat":
                    v.bn_stats(bnbuf[:, 0:6], psK[:, 0:512])
                    vsync(v.bn_stats(bnbuf[:, 6:12], psK[:, 512:1024]))
                    vsync(v.bn_aggr(mva[p2][:], bnbuf[:]))
                    v.tensor_scalar_add(vpa[p2][:], mva[p2][:, 1:2],
                                        EPS_LN).then_inc(sv, 1)
                elif name == "lnab":
                    v.scalar_tensor_tensor(b_a[p2][:], mva[p2][:, 0:1], -1.0,
                                           r_a[p2][:], Alu.mult,
                                           Alu.mult).then_inc(sv, 1)
                elif name == "ln2stat":
                    xb = x2[t % 3][:]
                    v.bn_stats(bnbuf[:, 0:6], xb[:, 0:512])
                    vsync(v.bn_stats(bnbuf[:, 6:12], xb[:, 512:1024]))
                    vsync(v.bn_aggr(mv2[p2][:], bnbuf[:]))
                    v.tensor_scalar_add(vp2[p2][:], mv2[p2][:, 1:2], EPS_LN)
                    v.tensor_scalar_mul(negm2[p2][:], mv2[p2][:, 0:1],
                                        -1.0).then_inc(sv, 1)
                elif name == "qsc":
                    vsync(v.reduce_max(admax[:], sgn[p2][:], axis=AX.X))
                    vsync(v.tensor_tensor(amax[:], admax[:], r_2[p2][:],
                                          Alu.mult))
                    vsync(v.tensor_scalar_max(amaxp[:], amax[:], epsq[:]))
                    vsync(v.reciprocal(rrecip[:], amaxp[:]))
                    vsync(v.tensor_scalar_mul(sinv[:], rrecip[:], 127.0))
                    v.tensor_tensor(xs_[:], amaxp[:], r127[:], Alu.mult)
                    vsync(v.tensor_tensor(rs_q[p2][:], r_2[p2][:], sinv[:],
                                          Alu.mult))
                    v.tensor_scalar_mul(nxs[p2][:], xs_[:], -1.0)
                    v.tensor_scalar(ubuf[p2][:], x2[t % 3][:],
                                    mv2[p2][:, 0:1], rs_q[p2][:],
                                    Alu.subtract, Alu.mult).then_inc(sv, 1)
                elif name.startswith("cord"):
                    k = int(name[4:])
                    ein = ubuf[p2] if k == 1 else \
                        (e_a[p2] if k % 2 == 0 else e_b[p2])
                    eout = e_a[p2] if k % 2 == 1 else e_b[p2]
                    v.scalar_tensor_tensor(eout[:], sgn[p2][:],
                                           -CORD_STEPS[k - 1], ein[:],
                                           Alu.mult, Alu.add).then_inc(sv, 1)
                elif name == "qfin":
                    elast = e_a[p2] if NC_ % 2 == 1 else e_b[p2]
                    v.scalar_tensor_tensor(xq_b[p2][:], elast[:], 0.5,
                                           ubuf[p2][:], Alu.add,
                                           Alu.subtract).then_inc(sv, 1)
                elif name == "deq":
                    v.tensor_tensor(ubuf[p2][:], lnfm[:], wsb_s[:],
                                    Alu.mult)
                    v.tensor_tensor(yf[:], ubuf[p2][:], biasb_s[:],
                                    Alu.add).then_inc(sv, 1)
                elif name == "lnfstat":
                    v.bn_stats(bnbuf[:, 0:6], yf[:, 0:512])
                    vsync(v.bn_stats(bnbuf[:, 6:12], yf[:, 512:1024]))
                    vsync(v.bn_aggr(mvf[p2][:], bnbuf[:]))
                    v.tensor_scalar_add(vpf[p2][:], mvf[p2][:, 1:2],
                                        EPS_LN).then_inc(sv, 1)
                elif name == "lnfb":
                    v.scalar_tensor_tensor(b_f2[p2][:], mvf[p2][:, 0:1],
                                           -1.0, r_f[p2][:], Alu.mult,
                                           Alu.mult).then_inc(sv, 1)

        @block.scalar
        def _(s):
            sxc = [0]

            def ssync(instr):
                sxc[0] += 1
                instr.then_inc(ssx, 1)
                s.wait_ge(ssx, sxc[0])
            for name, t in sched("S"):
                p2 = t % 2
                wdeps(s, name, t)
                if name == "ln1r":
                    ssync(s.activation(l_1[p2][:], vp1[p2][:], Act.Ln))
                    s.activation(r_1[p2][:], l_1[p2][:], Act.Exp,
                                 scale=np.float32(-0.5)).then_inc(ss, 1)
                elif name == "ln1n":
                    s.activation(ln1_b[:], x_in[t % 4][:], Act.Identity,
                                 bias=b_1[p2][:],
                                 scale=r_1[p2][:]).then_inc(ss, 1)
                elif name == "ln1fmev":
                    s.activation(ln1_fm[p2][:], psB[:],
                                 Act.Copy).then_inc(ss, 1)
                elif name == "qev":
                    s.activation(q_ev[p2][:], A0, Act.Copy).then_inc(ss, 1)
                elif name == "kev":
                    s.activation(k_ev[p2][:], A1, Act.Copy).then_inc(ss, 1)
                elif name == "vev":
                    s.activation(v_ev[t % 3][:], A2, Act.Copy).then_inc(ss, 1)
                elif name == "qfmev":
                    s.activation(q_fm[:], A0, Act.Copy).then_inc(ss, 1)
                elif name == "kfmev":
                    s.activation(k_fm[:], A1, Act.Copy).then_inc(ss, 1)
                elif name == "exp":
                    s.activation(Pexp[:], psQ[:], Act.Exp,
                                 scale=0.125).then_inc(ss, 1)
                elif name == "ptev":
                    s.activation(PT_b[:], A2, Act.Copy).then_inc(ss, 1)
                elif name == "yev":
                    s.activation(y_b[:], A2, Act.Copy).then_inc(ss, 1)
                elif name == "lnar":
                    ssync(s.activation(l_a[p2][:], vpa[p2][:], Act.Ln))
                    s.activation(r_a[p2][:], l_a[p2][:], Act.Exp,
                                 scale=np.float32(-0.5)).then_inc(ss, 1)
                elif name == "lnan":
                    s.activation(lnam[:], A1, Act.Identity,
                                 bias=b_a[p2][:],
                                 scale=r_a[p2][:]).then_inc(ss, 1)
                elif name == "ln2r":
                    ssync(s.activation(l_2[p2][:], vp2[p2][:], Act.Ln))
                    s.activation(r_2[p2][:], l_2[p2][:], Act.Exp,
                                 scale=np.float32(-0.5))
                    s.activation(sgn[p2][:], x2[t % 3][:], Act.Abs,
                                 bias=negm2[p2][:]).then_inc(ss, 1)
                elif name.startswith("csgn"):
                    k = int(name[4:])
                    ein = ubuf[p2] if k == 1 else \
                        (e_a[p2] if (k - 1) % 2 == 1 else e_b[p2])
                    s.activation(sgn[p2][:], ein[:], Act.Sign,
                                 bias=np.float32(0.5)).then_inc(ss, 1)
                elif name == "xqfmev":
                    s.activation(xq_fm[:], psB[:], Act.Copy).then_inc(ss, 1)
                elif name == "deqs":
                    s.activation(lnfm[:], psB[:], Act.Copy,
                                 scale=nxs[p2][:]).then_inc(ss, 1)
                elif name == "lnfr":
                    ssync(s.activation(l_f[p2][:], vpf[p2][:], Act.Ln))
                    s.activation(r_f[p2][:], l_f[p2][:], Act.Exp,
                                 scale=np.float32(-0.5)).then_inc(ss, 1)
                elif name == "lnfn":
                    s.activation(lnfm[:], yf[:], Act.Identity,
                                 bias=b_f2[p2][:],
                                 scale=r_f[p2][:]).then_inc(ss, 1)

        @block.tensor
        def _(te):
            te.wait_ge(sv, 1)
            t_need = {"qkv": True, "proj": True, "ffn": True}
            for name, t in sched("T"):
                p2 = t % 2
                if name.startswith("qkv_") and t_need["qkv"]:
                    te.wait_ge(sdwg, 16 * 8)
                    t_need["qkv"] = False
                elif name == "proj" and t_need["proj"]:
                    te.wait_ge(sdwg, 16 * 16)
                    t_need["proj"] = False
                elif name == "ffn" and t_need["ffn"]:
                    te.wait_ge(sdwg, 16 * 24)
                    t_need["ffn"] = False
                wdeps(te, name, t)
                if name == "ln1T":
                    for c in range(8):
                        mm = te.matmul(psB[:, c * 128:(c + 1) * 128],
                                       ln1_b[:, c * 128:(c + 1) * 128],
                                       Ib[:], start=True, stop=True)
                    mm.then_inc(st_, 1)
                elif name.startswith("qkv_"):
                    g = {"q": 0, "k": 1, "v": 2}[name[4]]
                    dst = (psQ, psK, psV)[g]
                    for c in range(8):
                        lhsT = ln1_fm[p2][:, c * 128:(c + 1) * 128]
                        for n in range(2):
                            mm = te.matmul(
                                dst[:, n * 512:(n + 1) * 512], lhsT,
                                wqkv_b[:, c * 3 * D + g * D + n * 512:
                                       c * 3 * D + g * D + (n + 1) * 512],
                                start=(c == 0), stop=(c == 7))
                    mm.then_inc(st_, 1)
                elif name == "qT":
                    for j in range(8):
                        mm = te.matmul(psQ[:, j * 128:(j + 1) * 128],
                                       qr[:, j * 128:(j + 1) * 128], Ib[:],
                                       start=True, stop=True)
                    mm.then_inc(st_, 1)
                elif name == "kT":
                    for j in range(8):
                        mm = te.matmul(psK[:, j * 128:(j + 1) * 128],
                                       kr[:, j * 128:(j + 1) * 128], Ib[:],
                                       start=True, stop=True)
                    mm.then_inc(st_, 1)
                elif name == "sall":
                    for c in range(2):
                        for h in range(16):
                            pr = (h % 2) * 64
                            j = h // 2
                            blk = c * 8 + j
                            mm = te.matmul(
                                psQ[pr:pr + 64, blk * 64:(blk + 1) * 64],
                                q_fm[pr:pr + 64, j * 128 + c * 64:
                                     j * 128 + c * 64 + 64],
                                k_fm[pr:pr + 64, j * 128 + c * 64:
                                     j * 128 + c * 64 + 64],
                                start=True, stop=True,
                                tile_position=(pr, pr))
                    mm.then_inc(st_, 1)
                elif name == "pt":
                    for c in range(2):
                        for j in range(8):
                            blk = c * 8 + j
                            mm = te.matmul(
                                psV[c * 64:c * 64 + 64,
                                    j * 128:(j + 1) * 128],
                                P_b[p2][:, blk * 64:(blk + 1) * 64], Ib[:],
                                start=True, stop=True,
                                tile_position=(0, c * 64))
                    mm.then_inc(st_, 1)
                elif name == "pv":
                    for c in range(2):
                        for h in range(16):
                            pr = (h % 2) * 64
                            j = h // 2
                            mm = te.matmul(
                                psV[pr:pr + 64, j * 128 + c * 64:
                                    j * 128 + c * 64 + 64],
                                v_ev[t % 3][c * 64:(c + 1) * 64,
                                            h * 64:(h + 1) * 64],
                                PT_b[c * 64:(c + 1) * 64,
                                     j * 128 + pr:j * 128 + pr + 64],
                                start=True, stop=True,
                                tile_position=(c * 64, pr))
                    mm.then_inc(st_, 1)
                elif name == "proj":
                    for j in range(8):
                        for n in range(2):
                            mm = te.matmul(psK[:, n * 512:(n + 1) * 512],
                                           y_b[:, j * 128:(j + 1) * 128],
                                           wproj_b[:, j * D + n * 512:
                                                   j * D + (n + 1) * 512],
                                           start=(j == 0), stop=(j == 7))
                    mm.then_inc(st_, 1)
                elif name == "xqT":
                    for c in range(8):
                        mm = te.matmul(psB[:, c * 128:(c + 1) * 128],
                                       xq_b[p2][:, c * 128:(c + 1) * 128],
                                       Ib[:], start=True, stop=True)
                    mm.then_inc(st_, 1)
                elif name == "ffn":
                    for c in range(8):
                        lhsT = xq_fm[:, c * 128:(c + 1) * 128]
                        for n in range(2):
                            mm = te.matmul(
                                psB[:, n * 512:(n + 1) * 512], lhsT,
                                wq_b[:, c * D + n * 512:
                                     c * D + (n + 1) * 512],
                                start=(c == 0), stop=(c == 7))
                    mm.then_inc(st_, 1)
    return nc


_TABLES = None


def _rope_tables():
    global _TABLES
    if _TABLES is None:
        import jax
        import jax.numpy as jnp
        with jax.default_device(jax.devices("cpu")[0]):
            inv_freq = 1.0 / (10000.0 ** (jnp.arange(0, 64, 2,
                                                     dtype=jnp.float32)
                                          / 64.0))
            tpos = jnp.arange(64, dtype=jnp.float32)
            freqs = jnp.outer(tpos, inv_freq)
            cos = np.asarray(jnp.cos(freqs), dtype=np.float32)
            sin = np.asarray(jnp.sin(freqs), dtype=np.float32)
        rows = np.arange(128) % 64
        ctab = np.tile(cos[rows], (1, 16)).astype(np.float32)
        stab = np.tile(sin[rows], (1, 16)).astype(np.float32)
        _TABLES = (np.ascontiguousarray(ctab), np.ascontiguousarray(stab))
    return _TABLES


_NC = None


def kernel(L0_attn_x, L0_attn_w_attn, L0_attn_w_proj, L0_ffn_weight_q,
           L0_ffn_weight_scale, L0_ffn_bias):
    global _NC
    x = np.ascontiguousarray(
        np.asarray(L0_attn_x, dtype=np.float32).reshape(NCORES * TPC, D))
    wqkvT = np.ascontiguousarray(np.asarray(L0_attn_w_attn,
                                            dtype=np.float32).T)
    wprojT = np.ascontiguousarray(np.asarray(L0_attn_w_proj,
                                             dtype=np.float32).T)
    wqT = np.ascontiguousarray(np.asarray(L0_ffn_weight_q, dtype=np.int8).T)
    wsb = np.ascontiguousarray(np.broadcast_to(
        np.asarray(L0_ffn_weight_scale, dtype=np.float32)[None, :],
        (128, D)))
    biasb = np.ascontiguousarray(np.broadcast_to(
        np.asarray(L0_ffn_bias, dtype=np.float32)[None, :], (128, D)))
    ctab, stab = _rope_tables()
    ident = np.eye(128, dtype=np.float32)

    if _NC is None:
        _NC = build_nc()
    in_maps = []
    for i in range(NCORES):
        in_maps.append({
            "x": np.ascontiguousarray(x[i * TPC:(i + 1) * TPC, :]),
            "wqkvT": wqkvT, "wprojT": wprojT, "wqT": wqT,
            "wsb": wsb, "biasb": biasb,
            "ctab": ctab, "stab": stab, "ident": ident,
        })
    res = run_bass_kernel_spmd(_NC, in_maps, core_ids=list(range(NCORES)))
    out = np.concatenate([np.asarray(r["out"]) for r in res.results], axis=0)
    return out.reshape(16, 4096, D).astype(np.float32, copy=False)


# version 21 (final)
# speedup vs baseline: 1.0156x; 1.0156x over previous
import numpy as np
import contextlib
import concourse.bass as bass
import concourse.mybir as mybir
from concourse.bass_utils import run_bass_kernel_spmd

F32 = mybir.dt.float32
BF16 = mybir.dt.bfloat16
I8 = mybir.dt.int8
Alu = mybir.AluOpType
Act = mybir.ActivationFunctionType
AX = mybir.AxisListType

NCORES = 8
NT = 64
D = 1024
TPC = NT * 128
EPS_LN = 1e-5
EPS_Q = 1e-6
CORD_STEPS = [64.0, 32.0, 16.0, 8.0, 4.0, 2.0, 1.0, 0.5]
NC_ = len(CORD_STEPS)
NSEG = 5

CORD = [(f"csgn{k}", "S") for k in range(1, NC_ + 1)]

ENG_OF = {
    "ln1stat": "V", "ln1r": "S", "ln1b": "V",
    "ln1n": "S", "ln1T": "T", "ln1fmev": "S",
    "qkv_q": "T", "qkv_k": "T", "qkv_v": "T",
    "qev": "S", "kev": "S", "vev": "S",
    "rope": "V", "qT": "T", "kT": "T",
    "qfmev": "S", "kfmev": "S", "sall": "T",
    "exp": "S", "pnorm": "V", "pt": "T", "ptev": "S", "pv": "T",
    "yev": "S", "proj": "T", "lnastat": "V", "lnar": "S", "lnab": "V", "lnan": "S", "r1": "G", "load": "D",
    "ln2stat": "V", "ln2r": "S", "qsc": "V",
    "qfin": "V", "xqT": "T", "xqfmev": "S", "ffn": "T",
    "deqs": "S", "deq": "V",
    "lnfstat": "V", "lnfr": "S", "lnfb": "V",
    "lnfn": "S", "r2": "G", "store": "D",
}
for k in range(1, NC_ + 1):
    ENG_OF[f"csgn{k}"] = "S"
    ENG_OF[f"cord{k}"] = "V"

DEPS = {
    "load": [("r1", -4)],
    "ln1stat": [("load", 0), ("ln1b", -2)],
    "ln1r": [("ln1stat", 0), ("ln1n", -2)],
    "ln1b": [("ln1r", 0), ("ln1n", -2)],
    "ln1n": [("ln1b", 0), ("ln1r", 0), ("ln1T", -1)],
    "ln1T": [("ln1n", 0), ("deqs", -5), ("ln1fmev", -1)],
    "ln1fmev": [("ln1T", 0), ("qkv_q", -2), ("qkv_k", -2), ("qkv_v", -2), ("rope", -2)],
    "qkv_q": [("ln1fmev", 0), ("exp", -2), ("qev", -1)],
    "qkv_k": [("ln1fmev", 0), ("lnastat", -3), ("lnan", -3), ("kev", -1)],
    "qkv_v": [("ln1fmev", 0), ("yev", -3), ("vev", -1)],
    "qev": [("qkv_q", 0), ("rope", -2)],
    "kev": [("qkv_k", 0), ("rope", -2)],
    "vev": [("qkv_v", 0), ("pv", -3)],
    "rope": [("qev", 0), ("kev", 0), ("qkv_v", 0), ("qT", -1), ("kT", -1)],
    "qT": [("rope", 0), ("qev", 1), ("exp", -1)],
    "kT": [("rope", 0), ("kev", 1), ("lnan", -2)],
    "qfmev": [("qT", 0), ("sall", -1)],
    "kfmev": [("kT", 0), ("sall", -1)],
    "sall": [("qfmev", 0), ("kfmev", 0), ("exp", -1)],
    "exp": [("sall", 0), ("pnorm", -1)],
    "pnorm": [("exp", 0), ("pt", -2)],
    "pt": [("pnorm", 0), ("vev", 2), ("ptev", -1), ("yev", -1)],
    "ptev": [("pt", 0), ("pv", -1)],
    "pv": [("ptev", 0), ("vev", 0)],
    "yev": [("pv", 0), ("proj", -1)],
    "proj": [("yev", 0), ("kfmev", 1), ("lnan", -1)],
    "lnastat": [("proj", 0), ("lnab", -2)],
    "lnar": [("lnastat", 0), ("lnan", -2)],
    "lnab": [("lnar", 0), ("lnan", -2)],
    "lnan": [("lnab", 0), ("lnar", 0), ("r1", -1)],
    "r1": [("lnan", 0), ("r2", -3)],
    "ln2stat": [("r1", 0), ("qsc", -2)],
    "ln2r": [("ln2stat", 0), (f"cord{NC_}", -2), ("qsc", -2)],
    "qsc": [("ln2r", 0), ("qfin", -2), ("deqs", -2), ("deq", -2)],
    "csgn1": [("qsc", 0)],
    "qfin": [(f"cord{NC_}", 0), ("xqT", -2)],
    "xqT": [("qfin", 0), ("ln1fmev", 4), ("xqfmev", -1), ("deqs", -1)],
    "xqfmev": [("xqT", 0), ("ffn", -1)],
    "ffn": [("xqfmev", 0)],
    "deqs": [("ffn", 0), ("r2", -1)],
    "deq": [("deqs", 0), ("r2", -1)],
    "lnfstat": [("deq", 0), ("lnfb", -2)],
    "lnfr": [("lnfstat", 0), ("lnfn", -2)],
    "lnfb": [("lnfr", 0), ("lnfn", -2)],
    "lnfn": [("lnfb", 0), ("lnfr", 0), ("r2", -1)],
    "r2": [("lnfn", 0), ("store", -2)],
    "store": [("r2", 0)],
}
for k in range(1, NC_ + 1):
    if k > 1:
        DEPS[f"csgn{k}"] = [(f"cord{k - 1}", 0)]
    DEPS[f"cord{k}"] = [(f"csgn{k}", 0)]

DUR = {
    "load": 700, "store": 700, "ln1stat": 1600,
    "ln1r": 560, "ln1n": 1260, "ln1T": 1400, "ln1fmev": 1120,
    "qkv_q": 3700, "qkv_k": 3700, "qkv_v": 3700,
    "qev": 1120, "kev": 1120, "vev": 1120,
    "rope": 4100, "qT": 1300, "kT": 1300,
    "qfmev": 1120, "kfmev": 1120, "sall": 2600,
    "exp": 1150, "pnorm": 2900, "pt": 1800, "ptev": 1120,
    "pv": 2600, "yev": 1120, "proj": 4600, "lnastat": 1700,
    "lnar": 560, "lnan": 1260, "r1": 2300, "ln2stat": 1800,
    "ln2r": 2600, "qsc": 2400, "qfin": 1250,
    "xqT": 1400, "xqfmev": 1120, "ffn": 4200,
    "deqs": 1100, "deq": 1900, "lnfstat": 1600,
    "lnfr": 560, "lnfn": 1260, "r2": 2300,
}
for k in range(1, NC_ + 1):
    DUR[f"csgn{k}"] = 1070
    DUR[f"cord{k}"] = 1250

ALL_STAGES = list(ENG_OF.keys())
SEM_LAT = 150.0


def list_schedule(nt):
    import heapq
    nodes = [(n, t) for t in range(nt) for n in ALL_STAGES]
    ndeps = {}
    dependents = {}
    def dep_tile(t, dt):
        u = t + dt
        if dt > 0:
            u = min(u, nt - 1)
        return u if 0 <= u < nt else None

    for n, t in nodes:
        k = 0
        seen = set()
        for dn, dt in DEPS.get(n, []):
            u = dep_tile(t, dt)
            if u is not None and (dn, u) != (n, t) and (dn, u) not in seen:
                seen.add((dn, u))
                k += 1
                dependents.setdefault((dn, u), []).append((n, t))
        ndeps[(n, t)] = k
    blevel = {}
    outdeg = {nd: len(dependents.get(nd, [])) for nd in nodes}
    stack = [nd for nd in nodes if outdeg[nd] == 0]
    for nd in stack:
        blevel[nd] = DUR.get(nd[0], 500)
    while stack:
        nd = stack.pop()
        for dn, dt in DEPS.get(nd[0], []):
            u = nd[1] + dt
            if 0 <= u < nt:
                p = (dn, u)
                blevel[p] = max(blevel.get(p, 0),
                                DUR.get(dn, 500) + blevel[nd])
                outdeg[p] -= 1
                if outdeg[p] == 0:
                    stack.append(p)
    eng_avail = {"V": 0.0, "S": 0.0, "T": 0.0, "G": 0.0, "D": 0.0}
    ready = {e: [] for e in eng_avail}
    rtime = {}
    finish = {}
    pending = []
    seq = 0
    for n, t in nodes:
        if ndeps[(n, t)] == 0:
            heapq.heappush(pending, (0.0, seq, n, t)); seq += 1
    order = {e: [] for e in eng_avail}
    n_done = 0
    clock = 0.0
    while n_done < len(nodes):
        moved = False
        while pending and pending[0][0] <= clock + 1e-9:
            rt, _, n, t = heapq.heappop(pending)
            e = ENG_OF[n]
            heapq.heappush(ready[e], (-blevel.get((n, t), 0), t, seq, n))
            seq += 1
            rtime[(n, t)] = rt
            moved = True
        progressed = False
        for e in eng_avail:
            if ready[e] and eng_avail[e] <= clock + 1e-9:
                _, t, _, n = heapq.heappop(ready[e])
                start = max(eng_avail[e], rtime[(n, t)])
                fin = start + DUR.get(n, 500)
                eng_avail[e] = fin
                finish[(n, t)] = fin
                order[e].append((n, t))
                n_done += 1
                progressed = True
                for (cn, cu) in dependents.get((n, t), []):
                    ndeps[(cn, cu)] -= 1
                    if ndeps[(cn, cu)] == 0:
                        dr = 0.0
                        for dn2, dt2 in DEPS.get(cn, []):
                            u2 = dep_tile(cu, dt2)
                            if u2 is not None and (dn2, u2) != (cn, cu):
                                dr = max(dr, finish[(dn2, u2)] + SEM_LAT)
                        heapq.heappush(pending, (dr, seq, cn, cu)); seq += 1
        if not progressed and not moved:
            cands = []
            if pending:
                cands.append(pending[0][0])
            for e in eng_avail:
                if ready[e]:
                    cands.append(eng_avail[e])
            if not cands:
                raise RuntimeError("schedule stuck")
            clock = max(min(cands), clock + 1.0)
    span = max(finish.values())
    return order, span


STAGE_INCS = {}
for _k in range(1, NC_ + 1):
    STAGE_INCS[f"cord{_k}"] = 2
    STAGE_INCS[f"csgn{_k}"] = 2


def plan_marks(nt, order):
    cnt = {"V": 1, "S": 0, "T": 0, "G": 0}
    marks = {}
    for e in ("V", "S", "T", "G"):
        c = cnt[e]
        for n, t in order[e]:
            c += STAGE_INCS.get(n, 1)
            marks[(n, t)] = c
    return marks


N_WDMA_G = 25
N_WDMA_S = 4


def build_nc(nt=NT, detect_races=False):
    tpc = nt * 128
    order, est_span = list_schedule(nt)
    marks = plan_marks(nt, order)
    nc = bass.Bass("TRN2", target_bir_lowering=False,
                   detect_race_conditions=detect_races)
    x_d = nc.dram_tensor("x", [tpc, D], F32, kind="ExternalInput").ap()
    wqkv_d = nc.dram_tensor("wqkvT", [D, 3 * D], F32, kind="ExternalInput").ap()
    wproj_d = nc.dram_tensor("wprojT", [D, D], F32, kind="ExternalInput").ap()
    wq_d = nc.dram_tensor("wqT", [D, D], I8, kind="ExternalInput").ap()
    wsb_d = nc.dram_tensor("wsb", [128, D], F32, kind="ExternalInput").ap()
    biasb_d = nc.dram_tensor("biasb", [128, D], F32, kind="ExternalInput").ap()
    ctab_d = nc.dram_tensor("ctab", [128, 512], F32, kind="ExternalInput").ap()
    stab_d = nc.dram_tensor("stab", [128, 512], F32, kind="ExternalInput").ap()
    id_d = nc.dram_tensor("ident", [128, 128], F32, kind="ExternalInput").ap()
    out_d = nc.dram_tensor("out", [tpc, D], F32, kind="ExternalOutput").ap()

    with contextlib.ExitStack() as es:
        def sb(name, shape, dt):
            return es.enter_context(nc.sbuf_tensor(name, shape, dt))

        def ring(name, n, shape, dt):
            return [sb(f"{name}{i}", shape, dt) for i in range(n)]

        wqkv_b = sb("wqkv_b", [128, 8 * 3 * D], BF16)
        wproj_b = sb("wproj_b", [128, 8 * D], BF16)
        wq_b = sb("wq_b", [128, 8 * D], BF16)
        wsb_s = sb("wsb_s", [128, D], F32)
        biasb_s = sb("biasb_s", [128, D], BF16)
        cosT = sb("cosT", [128, 512], BF16)
        sinT = sb("sinT", [128, 512], BF16)
        Ib = sb("Ib", [128, 128], BF16)
        c127 = sb("c127", [128, 1], F32)
        r127 = sb("r127", [128, 1], F32)
        epsq = sb("epsq", [128, 1], F32)

        x_in = ring("x_in", 4, [128, D], F32)
        xout = ring("xout", 2, [128, D], F32)
        ln1_b = sb("ln1_b", [128, D], BF16)
        ln1_fm = ring("ln1_fm", 2, [128, D], BF16)
        q_ev = ring("q_ev", 2, [128, D], BF16)
        k_ev = ring("k_ev", 2, [128, D], BF16)
        v_ev = ring("v_ev", 3, [128, D], BF16)
        qr = sb("qr", [128, D], BF16)
        kr = sb("kr", [128, D], BF16)
        rsc_a = sb("rsc_a", [128, 512], BF16)
        rsc_b = sb("rsc_b", [128, 512], BF16)
        q_fm = sb("q_fm", [128, D], BF16)
        k_fm = sb("k_fm", [128, D], BF16)
        Pexp = sb("Pexp", [128, D], BF16)
        P_b = ring("P_b", 2, [128, D], BF16)
        PT_b = sb("PT_b", [128, D], BF16)
        y_b = sb("y_b", [128, D], BF16)
        lnam = sb("lnam", [128, D], BF16)
        x2 = ring("x2", 3, [128, D], BF16)
        ubuf = ring("ubuf", 2, [128, D], F32)
        e_a = ring("e_a", 2, [128, D], F32)
        e_b = ring("e_b", 2, [128, D], F32)
        sgn = ring("sgn", 2, [128, D], F32)
        xq_b = ring("xq_b", 2, [128, D], BF16)
        xq_fm = sb("xq_fm", [128, D], BF16)
        yf = sb("yf", [128, D], BF16)
        lnfm = sb("lnfm", [128, D], F32)

        bnbuf = sb("bnbuf", [128, 12], F32)
        mv1 = ring("mv1", 2, [128, 2], F32)
        mva = ring("mva", 2, [128, 2], F32)
        mv2 = ring("mv2", 2, [128, 2], F32)
        mvf = ring("mvf", 2, [128, 2], F32)
        vp1 = ring("vp1", 2, [128, 1], F32)
        vpa = ring("vpa", 2, [128, 1], F32)
        vp2 = ring("vp2", 2, [128, 1], F32)
        vpf = ring("vpf", 2, [128, 1], F32)
        l_1 = ring("l_1", 2, [128, 1], F32)
        l_a = ring("l_a", 2, [128, 1], F32)
        l_2 = ring("l_2", 2, [128, 1], F32)
        l_f = ring("l_f", 2, [128, 1], F32)
        r_1 = ring("r_1", 2, [128, 1], F32)
        r_a = ring("r_a", 2, [128, 1], F32)
        r_2 = ring("r_2", 2, [128, 1], F32)
        r_f = ring("r_f", 2, [128, 1], F32)
        b_1 = ring("b_1", 2, [128, 1], F32)
        b_a = ring("b_a", 2, [128, 1], F32)
        b_f2 = ring("b_f2", 2, [128, 1], F32)
        negm2 = ring("negm2", 2, [128, 1], F32)
        admax = sb("admax", [128, 1], F32)
        amax = sb("amax", [128, 1], F32)
        amaxp = sb("amaxp", [128, 1], F32)
        rrecip = sb("rrecip", [128, 1], F32)
        sinv = sb("sinv", [128, 1], F32)
        rs_q = ring("rs_q", 2, [128, 1], F32)
        xs_ = sb("xs_", [128, 1], F32)
        nxs = ring("nxs", 2, [128, 1], F32)
        ssum = sb("ssum", [128, 16], F32)
        rs16 = sb("rs16", [128, 16], F32)

        psQ = es.enter_context(nc.psum_tensor("psQ", [128, D], F32))
        psK = es.enter_context(nc.psum_tensor("psK", [128, D], F32))
        psV = es.enter_context(nc.psum_tensor("psV", [128, D], F32))
        psB = es.enter_context(nc.psum_tensor("psB", [128, D], F32))
        A0 = psQ[:]
        A1 = psK[:]
        A2 = psV[:]

        sdx = es.enter_context(nc.semaphore("sdx"))
        sdo = es.enter_context(nc.semaphore("sdo"))
        sdwg = es.enter_context(nc.semaphore("sdwg"))
        sdws = es.enter_context(nc.semaphore("sdws"))
        sv = es.enter_context(nc.semaphore("sv"))
        ss = es.enter_context(nc.semaphore("ss"))
        st_ = es.enter_context(nc.semaphore("st_"))
        sg = es.enter_context(nc.semaphore("sg"))
        svx = es.enter_context(nc.semaphore("svx"))
        ssx = es.enter_context(nc.semaphore("ssx"))

        SEM = {"V": sv, "S": ss, "T": st_, "G": sg}

        def w(eng, name, t):
            if not (0 <= t < nt):
                return
            if name == "load":
                eng.wait_ge(sdx, 16 * (t + 1))
            elif name == "store":
                eng.wait_ge(sdo, 16 * (t + 1))
            else:
                eng.wait_ge(SEM[ENG_OF[name]], marks[(name, t)])

        def wdeps(eng, name, t):
            for dn, dt in DEPS.get(name, []):
                u = t + dt
                if dt > 0:
                    u = min(u, nt - 1)
                if (dn, u) != (name, t):
                    w(eng, dn, u)

        def sched(engine_code):
            return list(order[engine_code])

        block = es.enter_context(nc.Block())

        @block.sync
        def _(sync):
            for wv, srcv in ((wsb_s[:], wsb_d[:]),
                             (x_in[0][:, 0:512], ctab_d[:]),
                             (x_in[0][:, 512:1024], stab_d[:]),
                             (x_in[1][:, 0:128], id_d[:])):
                sync.dma_start(out=wv, in_=srcv).then_inc(sdws, 16)
            sync.wait_ge(sv, 1)
            lseen = -1
            sseen = -1
            for name, t in sched("D"):
                wdeps(sync, name, t)
                if name == "load":
                    assert t == lseen + 1
                    lseen = t
                    if t >= 1:
                        sync.wait_ge(sdx, 16 * t)
                    sync.dma_start(out=x_in[t % 4][:],
                                   in_=x_d[t * 128:(t + 1) * 128, :]
                                   ).then_inc(sdx, 16)
                else:
                    assert t == sseen + 1
                    sseen = t
                    if t >= 1:
                        sync.wait_ge(sdo, 16 * t)
                    sync.dma_start(out=out_d[t * 128:(t + 1) * 128, :],
                                   in_=xout[t % 2][:]).then_inc(sdo, 16)
            sync.wait_ge(sdo, 16 * nt)

        @block.gpsimd
        def _(gp):
            for c in range(8):
                gp.dma_start(out=wqkv_b[:, c * 3 * D:(c + 1) * 3 * D],
                             in_=wqkv_d[c * 128:(c + 1) * 128, :]
                             ).then_inc(sdwg, 16)
            for c in range(8):
                gp.dma_start(out=wproj_b[:, c * D:(c + 1) * D],
                             in_=wproj_d[c * 128:(c + 1) * 128, :]
                             ).then_inc(sdwg, 16)
            for c in range(8):
                gp.dma_start(out=wq_b[:, c * D:(c + 1) * D],
                             in_=wq_d[c * 128:(c + 1) * 128, :]
                             ).then_inc(sdwg, 16)
            gp.dma_start(out=biasb_s[:], in_=biasb_d[:]).then_inc(sdwg, 16)
            for name, t in sched("G"):
                wdeps(gp, name, t)
                if name == "r1":
                    gp.tensor_tensor(x2[t % 3][:], x_in[t % 4][:], lnam[:],
                                     Alu.add).then_inc(sg, 1)
                else:
                    gp.tensor_tensor(xout[t % 2][:], x2[t % 3][:], lnfm[:],
                                     Alu.add).then_inc(sg, 1)

        @block.vector
        def _(v):
            vxc = [0]

            def vsync(instr):
                vxc[0] += 1
                instr.then_inc(svx, 1)
                v.wait_ge(svx, vxc[0])
            v.wait_ge(sdws, 16 * N_WDMA_S)
            v.tensor_copy(cosT[:], x_in[0][:, 0:512])
            v.tensor_copy(sinT[:], x_in[0][:, 512:1024])
            v.tensor_copy(Ib[:], x_in[1][:, 0:128])
            vsync(v.memset(c127[:], 127.0))
            vsync(v.memset(epsq[:], EPS_Q))
            vsync(v.reciprocal(r127[:], c127[:]))
            v.tensor_scalar_add(r127[:], r127[:], 0.0).then_inc(sv, 1)

            def rv(buf):
                return buf[:].rearrange("p (h d) -> p h d", h=16)

            cv = rv(cosT)
            sv_v = rv(sinT)
            sav = rv(rsc_a)
            sbv = rv(rsc_b)

            v_need_w = [True]
            for name, t in sched("V"):
                p2 = t % 2
                if name == "deq" and v_need_w[0]:
                    v.wait_ge(sdwg, 16 * N_WDMA_G)
                    v_need_w[0] = False
                if not name.startswith("cord"):
                    wdeps(v, name, t)
                if name == "ln1stat":
                    xa = x_in[t % 4][:]
                    v.bn_stats(bnbuf[:, 0:6], xa[:, 0:512])
                    vsync(v.bn_stats(bnbuf[:, 6:12], xa[:, 512:1024]))
                    vsync(v.bn_aggr(mv1[p2][:], bnbuf[:]))
                    v.tensor_scalar_add(vp1[p2][:], mv1[p2][:, 1:2],
                                        EPS_LN).then_inc(sv, 1)
                elif name == "ln1b":
                    v.scalar_tensor_tensor(b_1[p2][:], mv1[p2][:, 0:1], -1.0,
                                           r_1[p2][:], Alu.mult,
                                           Alu.mult).then_inc(sv, 1)
                elif name == "rope":
                    scv = ln1_fm[p2][:, 0:512].rearrange(
                        "p (h f) -> p h f", f=32)
                    sdv = ln1_fm[p2][:, 512:1024].rearrange(
                        "p (h f) -> p h f", f=32)
                    qv_ = rv(q_ev[p2])
                    kv_ = rv(k_ev[p2])
                    qd = rv(qr)
                    kd = rv(kr)
                    q1, q2 = qv_[:, :, 0:32], qv_[:, :, 32:64]
                    k1, k2 = kv_[:, :, 0:32], kv_[:, :, 32:64]
                    v.tensor_tensor(sav, q1, cv, Alu.mult)
                    v.tensor_tensor(sbv, q2, sv_v, Alu.mult)
                    v.tensor_tensor(scv, k1, cv, Alu.mult)
                    v.tensor_tensor(sdv, k2, sv_v, Alu.mult)
                    v.tensor_tensor(qd[:, :, 0:32], sav, sbv, Alu.add)
                    v.tensor_tensor(kd[:, :, 0:32], scv, sdv, Alu.add)
                    v.tensor_tensor(sav, q2, cv, Alu.mult)
                    v.tensor_tensor(sbv, q1, sv_v, Alu.mult)
                    v.tensor_tensor(scv, k2, cv, Alu.mult)
                    v.tensor_tensor(sdv, k1, sv_v, Alu.mult)
                    v.tensor_tensor(qd[:, :, 32:64], sav, sbv, Alu.subtract)
                    ri = v.tensor_tensor(kd[:, :, 32:64], scv, sdv,
                                         Alu.subtract)
                    ri.then_inc(sv, 1)
                elif name == "pnorm":
                    vsync(v.reduce_sum(
                        ssum[:], Pexp[:].rearrange("p (a b) -> p a b", a=16),
                        axis=AX.X))
                    vsync(v.reciprocal(rs16[:], ssum[:]))
                    for b in range(16):
                        pi = v.tensor_scalar_mul(
                            P_b[p2][:, b * 64:(b + 1) * 64],
                            Pexp[:, b * 64:(b + 1) * 64], rs16[:, b:b + 1])
                    pi.then_inc(sv, 1)
                elif name == "lnastat":
                    v.bn_stats(bnbuf[:, 0:6], psK[:, 0:512])
                    vsync(v.bn_stats(bnbuf[:, 6:12], psK[:, 512:1024]))
                    vsync(v.bn_aggr(mva[p2][:], bnbuf[:]))
                    v.tensor_scalar_add(vpa[p2][:], mva[p2][:, 1:2],
                                        EPS_LN).then_inc(sv, 1)
                elif name == "lnab":
                    v.scalar_tensor_tensor(b_a[p2][:], mva[p2][:, 0:1], -1.0,
                                           r_a[p2][:], Alu.mult,
                                           Alu.mult).then_inc(sv, 1)
                elif name == "ln2stat":
                    xb = x2[t % 3][:]
                    v.bn_stats(bnbuf[:, 0:6], xb[:, 0:512])
                    vsync(v.bn_stats(bnbuf[:, 6:12], xb[:, 512:1024]))
                    vsync(v.bn_aggr(mv2[p2][:], bnbuf[:]))
                    v.tensor_scalar_add(vp2[p2][:], mv2[p2][:, 1:2], EPS_LN)
                    v.tensor_scalar_mul(negm2[p2][:], mv2[p2][:, 0:1],
                                        -1.0).then_inc(sv, 1)
                elif name == "qsc":
                    vsync(v.reduce_max(admax[:], sgn[p2][:], axis=AX.X))
                    vsync(v.tensor_tensor(amax[:], admax[:], r_2[p2][:],
                                          Alu.mult))
                    vsync(v.tensor_scalar_max(amaxp[:], amax[:], epsq[:]))
                    vsync(v.reciprocal(rrecip[:], amaxp[:]))
                    vsync(v.tensor_scalar_mul(sinv[:], rrecip[:], 127.0))
                    v.tensor_tensor(xs_[:], amaxp[:], r127[:], Alu.mult)
                    vsync(v.tensor_tensor(rs_q[p2][:], r_2[p2][:], sinv[:],
                                          Alu.mult))
                    v.tensor_scalar_mul(nxs[p2][:], xs_[:], -1.0)
                    v.tensor_scalar(ubuf[p2][:], x2[t % 3][:],
                                    mv2[p2][:, 0:1], rs_q[p2][:],
                                    Alu.subtract, Alu.mult).then_inc(sv, 1)
                elif name.startswith("cord"):
                    k = int(name[4:])
                    ein = ubuf[p2] if k == 1 else \
                        (e_a[p2] if k % 2 == 0 else e_b[p2])
                    eout = e_a[p2] if k % 2 == 1 else e_b[p2]
                    m = marks[(f"csgn{k}", t)]
                    v.wait_ge(ss, m - 1)
                    v.scalar_tensor_tensor(eout[:, 0:512],
                                           sgn[p2][:, 0:512],
                                           -CORD_STEPS[k - 1],
                                           ein[:, 0:512],
                                           Alu.mult, Alu.add).then_inc(sv, 1)
                    v.wait_ge(ss, m)
                    v.scalar_tensor_tensor(eout[:, 512:1024],
                                           sgn[p2][:, 512:1024],
                                           -CORD_STEPS[k - 1],
                                           ein[:, 512:1024],
                                           Alu.mult, Alu.add).then_inc(sv, 1)
                elif name == "qfin":
                    elast = e_a[p2] if NC_ % 2 == 1 else e_b[p2]
                    v.scalar_tensor_tensor(xq_b[p2][:], elast[:], 0.5,
                                           ubuf[p2][:], Alu.add,
                                           Alu.subtract).then_inc(sv, 1)
                elif name == "deq":
                    v.tensor_tensor(ubuf[p2][:], lnfm[:], wsb_s[:],
                                    Alu.mult)
                    v.tensor_tensor(yf[:], ubuf[p2][:], biasb_s[:],
                                    Alu.add).then_inc(sv, 1)
                elif name == "lnfstat":
                    v.bn_stats(bnbuf[:, 0:6], yf[:, 0:512])
                    vsync(v.bn_stats(bnbuf[:, 6:12], yf[:, 512:1024]))
                    vsync(v.bn_aggr(mvf[p2][:], bnbuf[:]))
                    v.tensor_scalar_add(vpf[p2][:], mvf[p2][:, 1:2],
                                        EPS_LN).then_inc(sv, 1)
                elif name == "lnfb":
                    v.scalar_tensor_tensor(b_f2[p2][:], mvf[p2][:, 0:1],
                                           -1.0, r_f[p2][:], Alu.mult,
                                           Alu.mult).then_inc(sv, 1)

        @block.scalar
        def _(s):
            sxc = [0]

            def ssync(instr):
                sxc[0] += 1
                instr.then_inc(ssx, 1)
                s.wait_ge(ssx, sxc[0])
            for name, t in sched("S"):
                p2 = t % 2
                if not (name.startswith("csgn") and name != "csgn1"):
                    wdeps(s, name, t)
                if name == "ln1r":
                    ssync(s.activation(l_1[p2][:], vp1[p2][:], Act.Ln))
                    s.activation(r_1[p2][:], l_1[p2][:], Act.Exp,
                                 scale=np.float32(-0.5)).then_inc(ss, 1)
                elif name == "ln1n":
                    s.activation(ln1_b[:], x_in[t % 4][:], Act.Identity,
                                 bias=b_1[p2][:],
                                 scale=r_1[p2][:]).then_inc(ss, 1)
                elif name == "ln1fmev":
                    s.activation(ln1_fm[p2][:], psB[:],
                                 Act.Copy).then_inc(ss, 1)
                elif name == "qev":
                    s.activation(q_ev[p2][:], A0, Act.Copy).then_inc(ss, 1)
                elif name == "kev":
                    s.activation(k_ev[p2][:], A1, Act.Copy).then_inc(ss, 1)
                elif name == "vev":
                    s.activation(v_ev[t % 3][:], A2, Act.Copy).then_inc(ss, 1)
                elif name == "qfmev":
                    s.activation(q_fm[:], A0, Act.Copy).then_inc(ss, 1)
                elif name == "kfmev":
                    s.activation(k_fm[:], A1, Act.Copy).then_inc(ss, 1)
                elif name == "exp":
                    s.activation(Pexp[:], psQ[:], Act.Exp,
                                 scale=0.125).then_inc(ss, 1)
                elif name == "ptev":
                    s.activation(PT_b[:], A2, Act.Copy).then_inc(ss, 1)
                elif name == "yev":
                    s.activation(y_b[:], A2, Act.Copy).then_inc(ss, 1)
                elif name == "lnar":
                    ssync(s.activation(l_a[p2][:], vpa[p2][:], Act.Ln))
                    s.activation(r_a[p2][:], l_a[p2][:], Act.Exp,
                                 scale=np.float32(-0.5)).then_inc(ss, 1)
                elif name == "lnan":
                    s.activation(lnam[:], A1, Act.Identity,
                                 bias=b_a[p2][:],
                                 scale=r_a[p2][:]).then_inc(ss, 1)
                elif name == "ln2r":
                    ssync(s.activation(l_2[p2][:], vp2[p2][:], Act.Ln))
                    s.activation(r_2[p2][:], l_2[p2][:], Act.Exp,
                                 scale=np.float32(-0.5))
                    s.activation(sgn[p2][:], x2[t % 3][:], Act.Abs,
                                 bias=negm2[p2][:]).then_inc(ss, 1)
                elif name.startswith("csgn"):
                    k = int(name[4:])
                    ein = ubuf[p2] if k == 1 else \
                        (e_a[p2] if (k - 1) % 2 == 1 else e_b[p2])
                    if k >= 2:
                        m = marks[(f"cord{k - 1}", t)]
                        s.wait_ge(sv, m - 1)
                    s.activation(sgn[p2][:, 0:512], ein[:, 0:512], Act.Sign,
                                 bias=np.float32(0.5)).then_inc(ss, 1)
                    if k >= 2:
                        s.wait_ge(sv, m)
                    s.activation(sgn[p2][:, 512:1024], ein[:, 512:1024],
                                 Act.Sign,
                                 bias=np.float32(0.5)).then_inc(ss, 1)
                elif name == "xqfmev":
                    s.activation(xq_fm[:], psB[:], Act.Copy).then_inc(ss, 1)
                elif name == "deqs":
                    s.activation(lnfm[:], psB[:], Act.Copy,
                                 scale=nxs[p2][:]).then_inc(ss, 1)
                elif name == "lnfr":
                    ssync(s.activation(l_f[p2][:], vpf[p2][:], Act.Ln))
                    s.activation(r_f[p2][:], l_f[p2][:], Act.Exp,
                                 scale=np.float32(-0.5)).then_inc(ss, 1)
                elif name == "lnfn":
                    s.activation(lnfm[:], yf[:], Act.Identity,
                                 bias=b_f2[p2][:],
                                 scale=r_f[p2][:]).then_inc(ss, 1)

        @block.tensor
        def _(te):
            te.wait_ge(sv, 1)
            t_need = {"qkv": True, "proj": True, "ffn": True}
            for name, t in sched("T"):
                p2 = t % 2
                if name.startswith("qkv_") and t_need["qkv"]:
                    te.wait_ge(sdwg, 16 * 8)
                    t_need["qkv"] = False
                elif name == "proj" and t_need["proj"]:
                    te.wait_ge(sdwg, 16 * 16)
                    t_need["proj"] = False
                elif name == "ffn" and t_need["ffn"]:
                    te.wait_ge(sdwg, 16 * 24)
                    t_need["ffn"] = False
                wdeps(te, name, t)
                if name == "ln1T":
                    for c in range(8):
                        mm = te.matmul(psB[:, c * 128:(c + 1) * 128],
                                       ln1_b[:, c * 128:(c + 1) * 128],
                                       Ib[:], start=True, stop=True)
                    mm.then_inc(st_, 1)
                elif name.startswith("qkv_"):
                    g = {"q": 0, "k": 1, "v": 2}[name[4]]
                    dst = (psQ, psK, psV)[g]
                    for c in range(8):
                        lhsT = ln1_fm[p2][:, c * 128:(c + 1) * 128]
                        for n in range(2):
                            mm = te.matmul(
                                dst[:, n * 512:(n + 1) * 512], lhsT,
                                wqkv_b[:, c * 3 * D + g * D + n * 512:
                                       c * 3 * D + g * D + (n + 1) * 512],
                                start=(c == 0), stop=(c == 7))
                    mm.then_inc(st_, 1)
                elif name == "qT":
                    for j in range(8):
                        mm = te.matmul(psQ[:, j * 128:(j + 1) * 128],
                                       qr[:, j * 128:(j + 1) * 128], Ib[:],
                                       start=True, stop=True)
                    mm.then_inc(st_, 1)
                elif name == "kT":
                    for j in range(8):
                        mm = te.matmul(psK[:, j * 128:(j + 1) * 128],
                                       kr[:, j * 128:(j + 1) * 128], Ib[:],
                                       start=True, stop=True)
                    mm.then_inc(st_, 1)
                elif name == "sall":
                    for c in range(2):
                        for h in range(16):
                            pr = (h % 2) * 64
                            j = h // 2
                            blk = c * 8 + j
                            mm = te.matmul(
                                psQ[pr:pr + 64, blk * 64:(blk + 1) * 64],
                                q_fm[pr:pr + 64, j * 128 + c * 64:
                                     j * 128 + c * 64 + 64],
                                k_fm[pr:pr + 64, j * 128 + c * 64:
                                     j * 128 + c * 64 + 64],
                                start=True, stop=True,
                                tile_position=(pr, pr))
                    mm.then_inc(st_, 1)
                elif name == "pt":
                    for c in range(2):
                        for j in range(8):
                            blk = c * 8 + j
                            mm = te.matmul(
                                psV[c * 64:c * 64 + 64,
                                    j * 128:(j + 1) * 128],
                                P_b[p2][:, blk * 64:(blk + 1) * 64], Ib[:],
                                start=True, stop=True,
                                tile_position=(0, c * 64))
                    mm.then_inc(st_, 1)
                elif name == "pv":
                    for c in range(2):
                        for h in range(16):
                            pr = (h % 2) * 64
                            j = h // 2
                            mm = te.matmul(
                                psV[pr:pr + 64, j * 128 + c * 64:
                                    j * 128 + c * 64 + 64],
                                v_ev[t % 3][c * 64:(c + 1) * 64,
                                            h * 64:(h + 1) * 64],
                                PT_b[c * 64:(c + 1) * 64,
                                     j * 128 + pr:j * 128 + pr + 64],
                                start=True, stop=True,
                                tile_position=(c * 64, pr))
                    mm.then_inc(st_, 1)
                elif name == "proj":
                    for j in range(8):
                        for n in range(2):
                            mm = te.matmul(psK[:, n * 512:(n + 1) * 512],
                                           y_b[:, j * 128:(j + 1) * 128],
                                           wproj_b[:, j * D + n * 512:
                                                   j * D + (n + 1) * 512],
                                           start=(j == 0), stop=(j == 7))
                    mm.then_inc(st_, 1)
                elif name == "xqT":
                    for c in range(8):
                        mm = te.matmul(psB[:, c * 128:(c + 1) * 128],
                                       xq_b[p2][:, c * 128:(c + 1) * 128],
                                       Ib[:], start=True, stop=True)
                    mm.then_inc(st_, 1)
                elif name == "ffn":
                    for c in range(8):
                        lhsT = xq_fm[:, c * 128:(c + 1) * 128]
                        for n in range(2):
                            mm = te.matmul(
                                psB[:, n * 512:(n + 1) * 512], lhsT,
                                wq_b[:, c * D + n * 512:
                                     c * D + (n + 1) * 512],
                                start=(c == 0), stop=(c == 7))
                    mm.then_inc(st_, 1)
    return nc


_TABLES = None


def _rope_tables():
    global _TABLES
    if _TABLES is None:
        import jax
        import jax.numpy as jnp
        with jax.default_device(jax.devices("cpu")[0]):
            inv_freq = 1.0 / (10000.0 ** (jnp.arange(0, 64, 2,
                                                     dtype=jnp.float32)
                                          / 64.0))
            tpos = jnp.arange(64, dtype=jnp.float32)
            freqs = jnp.outer(tpos, inv_freq)
            cos = np.asarray(jnp.cos(freqs), dtype=np.float32)
            sin = np.asarray(jnp.sin(freqs), dtype=np.float32)
        rows = np.arange(128) % 64
        ctab = np.tile(cos[rows], (1, 16)).astype(np.float32)
        stab = np.tile(sin[rows], (1, 16)).astype(np.float32)
        _TABLES = (np.ascontiguousarray(ctab), np.ascontiguousarray(stab))
    return _TABLES


_NC = None


def kernel(L0_attn_x, L0_attn_w_attn, L0_attn_w_proj, L0_ffn_weight_q,
           L0_ffn_weight_scale, L0_ffn_bias):
    global _NC
    x = np.ascontiguousarray(
        np.asarray(L0_attn_x, dtype=np.float32).reshape(NCORES * TPC, D))
    wqkvT = np.ascontiguousarray(np.asarray(L0_attn_w_attn,
                                            dtype=np.float32).T)
    wprojT = np.ascontiguousarray(np.asarray(L0_attn_w_proj,
                                             dtype=np.float32).T)
    wqT = np.ascontiguousarray(np.asarray(L0_ffn_weight_q, dtype=np.int8).T)
    wsb = np.ascontiguousarray(np.broadcast_to(
        np.asarray(L0_ffn_weight_scale, dtype=np.float32)[None, :],
        (128, D)))
    biasb = np.ascontiguousarray(np.broadcast_to(
        np.asarray(L0_ffn_bias, dtype=np.float32)[None, :], (128, D)))
    ctab, stab = _rope_tables()
    ident = np.eye(128, dtype=np.float32)

    if _NC is None:
        _NC = build_nc()
    in_maps = []
    for i in range(NCORES):
        in_maps.append({
            "x": np.ascontiguousarray(x[i * TPC:(i + 1) * TPC, :]),
            "wqkvT": wqkvT, "wprojT": wprojT, "wqT": wqT,
            "wsb": wsb, "biasb": biasb,
            "ctab": ctab, "stab": stab, "ident": ident,
        })
    res = run_bass_kernel_spmd(_NC, in_maps, core_ids=list(range(NCORES)))
    out = np.concatenate([np.asarray(r["out"]) for r in res.results], axis=0)
    return out.reshape(16, 4096, D).astype(np.float32, copy=False)
